# version 02 (final)
# baseline (speedup 1.0000x reference)
import math
import numpy as np
import ml_dtypes
import concourse.bass as bass
import concourse.mybir as mybir
from concourse.bass_utils import run_bass_kernel_spmd

F32 = mybir.dt.float32
BF16 = mybir.dt.bfloat16
I32 = mybir.dt.int32
AF = mybir.ActivationFunctionType
ALU = mybir.AluOpType
AX = mybir.AxisListType

D = 4096
S = 8192
NB = 16
HEADS = 32
HD = 128
SCALE = 1.0 / math.sqrt(HD)
EPS = 1e-5
NCOL = 8192
HG = 3


def alibi_slopes(n):
    start = 2.0 ** (-2.0 ** (-(np.log2(n) - 3)))
    return np.array([start ** (i + 1) for i in range(n)], dtype=np.float32)

SLOPES = alibi_slopes(HEADS)

def subtile_T(h):
    s = float(SLOPES[h])
    for T in (512, 256, 128, 64):
        if s * (T - 1) <= 60.0:
            return T
    return 64

SUBT = [subtile_T(h) for h in range(HEADS)]
HGROUPS = [list(range(h, min(h + HG, HEADS))) for h in range(0, HEADS, HG)]


def plan_assignment():
    KP = 3.7 / NB
    KA = 7.0 / 544.0
    KO = 10.5 / NB
    best = None
    import itertools
    for ws in itertools.product(range(1, 7), repeat=8):
        if sum(ws) != NB:
            continue
        E = [0]
        for w in ws:
            E.append(E[-1] + w)
        costs = []
        nparts = []
        for p in range(4):
            c0, c1 = 2 * p, 2 * p + 1
            own0, own1 = ws[c0], ws[c1]
            pend = E[2 * p + 2]
            R = pend - own0 - own1
            r0 = max(0, min(R, (R + own1 - own0 + 1) // 2))
            r1 = R - r0
            nparts += [own0 + r0, own1 + r1]
            for cc, n in ((c0, own0 + r0), (c1, own1 + r1)):
                a = KA * sum(4 * B + 4 for B in range(E[cc], E[cc + 1]))
                costs.append(KP * n + a + KO * ws[cc])
        m = max(costs)
        if best is None or m < best[0]:
            best = (m, ws, E, nparts)
    m, ws, E, nparts = best
    parts = []
    for p in range(4):
        c0, c1 = 2 * p, 2 * p + 1
        own0 = list(range(E[c0], E[c0 + 1]))
        own1 = list(range(E[c1], E[c1 + 1]))
        R = list(range(0, E[c0]))
        r0n = nparts[c0] - len(own0)
        parts.append(sorted(own0 + R[:r0n]))
        parts.append(sorted(own1 + R[r0n:]))
    own = [list(range(E[c], E[c + 1])) for c in range(8)]
    return E, own, parts

E_BOUND, OWN, PARTS = plan_assignment()
MAXW = max(len(o) for o in OWN)


def attention_schedule(c):
    recs = []
    col = 0
    for B in OWN[c]:
        g0 = B * 512
        nch = 4 * B + 4
        gl = []
        for hg in HGROUPS:
            ch = []
            for kc in range(nch):
                k0 = kc * 128
                heads = []
                for h in hg:
                    T = SUBT[h]
                    subs = []
                    for qs in range(g0, g0 + 512, T):
                        if qs + T <= k0:
                            subs.append((qs, T, "zero", -1))
                        elif qs >= k0 + 128:
                            subs.append((qs, T, "full", col)); col += 1
                        else:
                            subs.append((qs, T, "part", col)); col += 1
                    heads.append((h, subs))
                ch.append((kc, heads))
            gl.append((hg, ch))
        recs.append((B, nch, gl))
    assert col <= NCOL, col
    return recs, col


def build_alibi(c):
    recs, ncol = attention_schedule(c)
    alb = np.zeros((128, NCOL), dtype=np.float32)
    p = np.arange(128, dtype=np.float64)
    for B, nch, gl in recs:
        for hg, ch in gl:
            for kc, heads in ch:
                for h, subs in heads:
                    sl = float(SLOPES[h])
                    for qs, T, kind, col in subs:
                        if col < 0:
                            continue
                        qend = qs + T - 1
                        alb[:, col] = (sl * (kc * 128 + p - qend)).astype(np.float32)
    return alb


def mask_patterns():
    pats = {}
    tiles = []
    off = 0
    for c in range(8):
        recs, _ = attention_schedule(c)
        for B, nch, gl in recs:
            for hg, ch in gl:
                for kc, heads in ch:
                    k0 = kc * 128
                    for h, subs in heads:
                        for qs, T, kind, col in subs:
                            if kind != "part":
                                continue
                            key = (T, k0 - qs)
                            if key not in pats:
                                j = np.arange(T)
                                m = (qs + j[None, :] >= k0 + np.arange(128)[:, None]).astype(np.float32)
                                pats[key] = off
                                tiles.append(m.astype(ml_dtypes.bfloat16))
                                off += T
    arr = (np.concatenate(tiles, axis=1) if tiles
           else np.zeros((128, 64), dtype=ml_dtypes.bfloat16))
    return pats, arr

MASK_OFF, MASK_ARR = mask_patterns()
MASKW = int(math.ceil(MASK_ARR.shape[1] / 512.0) * 512)

AR_N = 81920

def _off():
    o = {}
    o["dq_pk"] = 0; o["dq_lo"] = 4096; o["dq_hi"] = 8192
    o["dq_lf"] = 12288; o["dq_hf"] = 16384; o["dq_wf"] = 20480
    o["dq_or"] = 28672; o["dq_wb"] = 36864
    o["xt0"] = 0; o["xt1"] = 8192; o["ht0"] = 16384; o["ht1"] = 20480
    o["sq"] = 24576
    o["wsb"] = 0; o["hT0"] = 32768; o["hT1"] = 49152
    o["xs"] = 65536; o["xo"] = 69632; o["qo"] = 73728
    o["kvw"] = 49152
    o["kt"] = 0; o["va"] = 8192; o["qt"] = 16384; o["at"] = 32768
    o["ktmp"] = 32768
    o["pt"] = 49152; o["alb"] = 53248; o["msk"] = 69632
    return o

OFF = _off()
assert OFF["msk"] + MASKW <= AR_N, (MASKW,)


class Em:
    def __init__(self):
        self.q = {k: [] for k in "tvsgy"}
        self.cnt = dict(DW=0, DG=0, DO=0, P=0, V=0, Sc=0, G=0)
        self.lastw = {}

    def emit(self, eng, fn, waits=(), inc=None, incn=1):
        ws = []
        for sem, val in waits:
            if val <= 0:
                continue
            key = (eng, sem)
            if self.lastw.get(key, -1) >= val:
                continue
            self.lastw[key] = val
            ws.append((sem, val))
        if inc is not None:
            self.cnt[inc] += incn
        self.q[eng].append((fn, ws, inc, incn))

    def c(self, sem):
        return self.cnt[sem]

    def barrier(self):
        return [(s, self.cnt[s]) for s in ("DW", "DG", "P", "V", "Sc", "G")]


def build_core(c, R):
    nc = R["nc"]
    em = Em()
    ar = R["ar"]; arf = R["arf"]

    def vb(name, w):
        o = OFF[name]
        return ar[:, o:o + w]

    def vf(name, w):
        o = OFF[name] // 2
        return arf[:, o:o + w]

    x_h = R["x_h"]; wq_h = R["wq_h"]; wo_h = R["wo_h"]; wkv_h = R["wkv_h"]
    pk_h = R["pk_h"]; sc_h = R["sc_h"]; or_h = R["or_h"]
    alb_h = R["alb_h"]; msk_h = R["msk_h"]; idn_h = R["idn_h"]; non_h = R["non_h"]
    out_h = R["out_h"]
    W_hbm = R["W_hbm"]; FW = R["FW"]; KV = R["KV"]; FK = R["FK"]
    H1 = R["H1"]; H2 = R["H2"]; X1 = R["X1"]; Qh = R["Qh"]; ATh = R["ATh"]
    X2 = R["X2"]; H3 = R["H3"]
    idn = R["idn"]; non = R["non"]; ones_bf = R["ones_bf"]; ones_f = R["ones_f"]
    stt = R["stt"]; rec = R["rec"]; bc = R["bc"]; fl = R["fl"]; fs = R["fs"]
    pb = R["pb"]; pbt = R["pbt"]

    parts = PARTS[c]
    own = OWN[c]
    slot = c & 1
    recs, _ = attention_schedule(c)

    em.emit("y", lambda e: e.dma_start(out=idn[:], in_=idn_h), inc="DW", incn=16)
    em.emit("y", lambda e: e.dma_start(out=non[0:1, :], in_=non_h), inc="DW", incn=16)
    em.emit("v", lambda e: e.memset(ones_bf[:], 1.0), inc="V")
    em.emit("v", lambda e: e.memset(ones_f[:], 1.0), inc="V")

    pkv = vb("dq_pk", 4096).bitcast(I32)
    lov = vb("dq_lo", 4096).bitcast(I32)
    hiv = vb("dq_hi", 4096).bitcast(I32)
    lfv = vf("dq_lf", 2048)
    hfv = vf("dq_hf", 2048)
    wfv = vf("dq_wf", 4096)
    orv = vf("dq_or", 4096)
    wbv = vb("dq_wb", 4096)
    scv = stt[:, 0:1]
    for oc in range(slot * 16, slot * 16 + 16):
        r0 = oc * 128
        wv = em.c("V")
        em.emit("y", lambda e, r0=r0: e.dma_start(out=pkv, in_=pk_h[r0:r0 + 128, :]),
                waits=[("V", wv)], inc="DW", incn=16)
        em.emit("y", lambda e, r0=r0: e.dma_start(out=orv, in_=or_h[r0:r0 + 128, :]),
                inc="DW", incn=16)
        em.emit("y", lambda e, r0=r0: e.dma_start(out=scv, in_=sc_h[r0:r0 + 128, :]),
                waits=[("V", wv)], inc="DW", incn=16)
        ld = em.c("DW")
        em.emit("v", lambda e: e.tensor_scalar(out=lov, in0=pkv, scalar1=15, scalar2=None,
                                               op0=ALU.bitwise_and),
                waits=[("DW", ld)], inc="V")
        em.emit("v", lambda e: e.tensor_scalar(out=hiv, in0=pkv, scalar1=4, scalar2=15,
                                               op0=ALU.logical_shift_right, op1=ALU.bitwise_and),
                inc="V")
        em.emit("v", lambda e: e.tensor_copy(out=lfv, in_=lov), waits=[("V", em.c("V"))], inc="V")
        em.emit("v", lambda e: e.tensor_copy(out=hfv, in_=hiv), inc="V")
        em.emit("v", lambda e: e.tensor_scalar(out=wfv[:, 0:4096:2], in0=lfv, scalar1=8.0,
                                               scalar2=scv, op0=ALU.subtract, op1=ALU.mult),
                waits=[("V", em.c("V"))], inc="V")
        em.emit("v", lambda e: e.tensor_scalar(out=wfv[:, 1:4096:2], in0=hfv, scalar1=8.0,
                                               scalar2=scv, op0=ALU.subtract, op1=ALU.mult),
                inc="V")
        em.emit("v", lambda e: e.tensor_add(out=wbv, in0=wfv, in1=orv),
                waits=[("V", em.c("V"))], inc="V")
        em.emit("y", lambda e, r0=r0: e.dma_start(out=W_hbm.ap()[r0:r0 + 128, :], in_=wbv),
                waits=[("V", em.c("V"))], inc="DW", incn=16)
    em.emit("g", lambda e: e.dma_start(out=FW.ap()[slot:slot + 1, :], in_=non[0:1, :]),
            waits=[("DW", em.c("DW"))], inc="DG", incn=16)
    em.q["g"].append(("POLL", FW.ap(), 2, "PL"))
    em.emit("g", lambda e: e.memset(fs[:], 0), inc="G")
    wpoll_g = em.c("G")

    def ln_pass(src_ap, dst_ap, tiles):
        xt = [vf("xt0", 4096), vf("xt1", 4096)]
        htl = [vb("ht0", 4096), vb("ht1", 4096)]
        sq = vf("sq", 4096)
        bar = em.barrier()
        last_use = [0, 0]
        for i, (sr, dr) in enumerate(tiles):
            par = i % 2
            xv, hv = xt[par], htl[par]
            st0 = 1 + par * 4
            em.emit("y", lambda e, sr=sr, xv=xv: e.dma_start(out=xv, in_=src_ap[sr:sr + 128, :]),
                    waits=bar + [("V", last_use[par])], inc="DW", incn=16)
            ld = em.c("DW")
            em.emit("v", lambda e, xv=xv, st0=st0: e.reduce_sum(out=stt[:, st0:st0 + 1], in_=xv, axis=AX.X),
                    waits=[("DW", ld)], inc="V")
            scw = em.c("Sc")
            em.emit("s", lambda e, xv=xv: e.activation(out=sq, in_=xv, func=AF.Square),
                    waits=[("DW", ld), ("V", em.c("V"))], inc="Sc")
            em.emit("v", lambda e, st0=st0: e.reduce_sum(out=stt[:, st0 + 1:st0 + 2], in_=sq, axis=AX.X),
                    waits=[("Sc", em.c("Sc"))], inc="V")
            em.emit("v", lambda e, st0=st0: e.tensor_scalar(out=stt[:, st0:st0 + 1], in0=stt[:, st0:st0 + 1],
                                                            scalar1=1.0 / D, scalar2=None, op0=ALU.mult),
                    waits=[("V", em.c("V"))], inc="V")
            em.emit("v", lambda e, st0=st0: e.tensor_scalar(out=stt[:, st0 + 1:st0 + 2],
                                                            in0=stt[:, st0 + 1:st0 + 2],
                                                            scalar1=1.0 / D, scalar2=None, op0=ALU.mult),
                    inc="V")
            em.emit("v", lambda e, st0=st0: e.tensor_mul(out=stt[:, st0 + 2:st0 + 3],
                                                         in0=stt[:, st0:st0 + 1], in1=stt[:, st0:st0 + 1]),
                    waits=[("V", em.c("V"))], inc="V")
            em.emit("v", lambda e, st0=st0: e.tensor_sub(out=stt[:, st0 + 1:st0 + 2],
                                                         in0=stt[:, st0 + 1:st0 + 2],
                                                         in1=stt[:, st0 + 2:st0 + 3]),
                    waits=[("V", em.c("V"))], inc="V")
            em.emit("v", lambda e, st0=st0: e.tensor_scalar(out=stt[:, st0 + 1:st0 + 2],
                                                            in0=stt[:, st0 + 1:st0 + 2],
                                                            scalar1=EPS, scalar2=None, op0=ALU.add),
                    waits=[("V", em.c("V"))], inc="V")
            vv = em.c("V")
            em.emit("s", lambda e, st0=st0: e.activation(out=stt[:, st0 + 1:st0 + 2],
                                                         in_=stt[:, st0 + 1:st0 + 2], func=AF.Ln),
                    waits=[("V", vv)], inc="Sc")
            em.emit("s", lambda e, st0=st0: e.activation(out=stt[:, st0 + 1:st0 + 2],
                                                         in_=stt[:, st0 + 1:st0 + 2], func=AF.Exp, scale=-0.5),
                    waits=[("Sc", em.c("Sc"))], inc="Sc")
            em.emit("v", lambda e, xv=xv, hv=hv, st0=st0: e.tensor_scalar(
                out=hv, in0=xv, scalar1=stt[:, st0:st0 + 1], scalar2=stt[:, st0 + 1:st0 + 2],
                op0=ALU.subtract, op1=ALU.mult),
                    waits=[("Sc", em.c("Sc"))], inc="V")
            last_use[par] = em.c("V")
            em.emit("y", lambda e, dr=dr, hv=hv: e.dma_start(out=dst_ap[dr:dr + 128, :], in_=hv),
                    waits=[("V", em.c("V"))], inc="DW", incn=16)

    ln_pass(x_h, H1.ap(), [(B * 512 + t * 128,) * 2 for B in parts for t in range(4)])

    def weight_pass(wload, blocks, hsrc_ap, hrow, drain):
        wsb = vb("wsb", 32768)
        hT = [vb("hT0", 16384), vb("hT1", 16384)]
        bar = em.barrier()
        bankv = {}
        hloaded = {}
        for qtr in range(4):
            wwait = wload(qtr, wsb, bar)
            for bi, B in enumerate(blocks):
                par = bi % 2
                hv = hT[par]
                pw = em.c("P")
                for ic in range(32):
                    em.emit("y", lambda e, B=B, hv=hv, ic=ic: e.dma_start(
                        out=hv[:, ic * 512:(ic + 1) * 512],
                        in_=hsrc_ap[hrow(B):hrow(B) + 512, ic * 128:(ic + 1) * 128],
                        transpose=True),
                            waits=bar + [("P", hloaded.get(par, 0))] if ic == 0 else (),
                            inc="DW", incn=16)
                hdw = em.c("DW")
                for t in range(4):
                    for o2 in range(2):
                        bank = (bi * 8 + t * 2 + o2) % 6
                        w0 = wwait + [("DW", hdw), ("V", bankv.get(bank, 0))]
                        for i in range(32):
                            em.emit("t", lambda e, i=i, t=t, o2=o2, bank=bank, hv=hv: e.matmul(
                                out=pb[bank][:],
                                lhsT=hv[:, i * 512 + t * 128:i * 512 + (t + 1) * 128],
                                rhs=wsb[:, i * 1024 + o2 * 512:i * 1024 + (o2 + 1) * 512],
                                start=(i == 0), stop=(i == 31)),
                                    waits=w0 if i == 0 else (), inc="P" if i == 31 else None)
                        drain(qtr, o2, bank, bi, B, t, em.c("P"), bankv)
                hloaded[par] = em.c("P")

    def w_from_whbm(qtr, wsb, bar):
        for ic in range(32):
            em.emit("y", lambda e, ic=ic, qtr=qtr: e.dma_start(
                out=wsb[:, ic * 1024:(ic + 1) * 1024],
                in_=W_hbm.ap()[qtr * 1024:(qtr + 1) * 1024, ic * 128:(ic + 1) * 128],
                transpose=True),
                    waits=bar + [("G", wpoll_g), ("P", em.c("P"))] if ic == 0 else (),
                    inc="DW", incn=16)
        return [("DW", em.c("DW"))]

    def ffn_drain(xsrc_ap, xrow, dst_ap, drow, half):
        xsv = vf("xs", 2048)
        xov = vf("xo", 2048)

        def drain(qtr, o2, bank, bi, B, t, pwait, bankv):
            dpar = (t * 2 + o2) % 4
            xv = xsv[:, dpar * 512:(dpar + 1) * 512]
            ov = xov[:, dpar * 512:(dpar + 1) * 512]
            ocol = qtr * 1024 + o2 * 512
            r0s = xrow(B) + t * 128
            r0d = drow(B) + t * 128
            em.emit("y", lambda e, r0s=r0s, ocol=ocol, xv=xv: e.dma_start(
                out=xv, in_=xsrc_ap[r0s:r0s + 128, ocol:ocol + 512]),
                    waits=[("V", bankv.get(("d", dpar), 0))], inc="DW", incn=16)
            xdw = em.c("DW")
            em.emit("v", lambda e, bank=bank, xv=xv, ov=ov, half=half: e.scalar_tensor_tensor(
                out=ov, in0=pb[bank][:], scalar=half, in1=xv, op0=ALU.mult, op1=ALU.add),
                    waits=[("P", pwait), ("DW", xdw)], inc="V")
            bankv[bank] = em.c("V")
            bankv[("d", dpar)] = em.c("V")
            sem = "DO" if dst_ap is out_h else "DW"
            em.emit("y", lambda e, r0d=r0d, ocol=ocol, ov=ov, sem=sem: e.dma_start(
                out=dst_ap[r0d:r0d + 128, ocol:ocol + 512], in_=ov),
                    waits=[("V", em.c("V"))], inc=sem, incn=16)
        return drain

    weight_pass(w_from_whbm, parts, H1.ap(), lambda B: B * 512,
                ffn_drain(x_h, lambda B: B * 512, X1.ap(), lambda B: B * 512, 0.5))

    ln_pass(X1.ap(), H2.ap(), [(B * 512 + t * 128,) * 2 for B in parts for t in range(4)])

    kvw = vb("kvw", 8192)
    bar = em.barrier()
    for ic in range(32):
        em.emit("g", lambda e, ic=ic: e.dma_start(
            out=kvw[:, ic * 256:(ic + 1) * 256],
            in_=wkv_h[ic * 128:(ic + 1) * 128, :]),
                waits=bar if ic == 0 else (), inc="DG", incn=16)
    kvw_dg = em.c("DG")
    hT0 = vb("hT0", 16384)
    qov = vb("qo", 2048)
    hl = 0
    scl = 0
    for bi, B in enumerate(parts):
        for ic in range(32):
            em.emit("y", lambda e, B=B, ic=ic: e.dma_start(
                out=hT0[:, ic * 512:(ic + 1) * 512],
                in_=H2.ap()[B * 512:(B + 1) * 512, ic * 128:(ic + 1) * 128],
                transpose=True),
                    waits=bar + [("P", hl)] if ic == 0 else (), inc="DW", incn=16)
        hdw = em.c("DW")
        for t in range(4):
            bank = t % 2
            for i in range(32):
                em.emit("t", lambda e, i=i, t=t, bank=bank: e.matmul(
                    out=pb[bank][:, 0:256],
                    lhsT=hT0[:, i * 512 + t * 128:i * 512 + (t + 1) * 128],
                    rhs=kvw[:, i * 256:(i + 1) * 256],
                    start=(i == 0), stop=(i == 31)),
                        waits=[("DW", hdw), ("DG", kvw_dg), ("Sc", scl)] if i == 0 else (),
                        inc="P" if i == 31 else None)
            pv = em.c("P")
            qv = qov[:, (t % 4) * 256:(t % 4) * 256 + 256]
            em.emit("s", lambda e, bank=bank, qv=qv: e.copy(out=qv, in_=pb[bank][:, 0:256]),
                    waits=[("P", pv), ("DW", em.c("DW"))], inc="Sc")
            scl = em.c("Sc")
            r0 = B * 512 + t * 128
            em.emit("y", lambda e, r0=r0, qv=qv: e.dma_start(out=KV.ap()[r0:r0 + 128, :], in_=qv),
                    waits=[("Sc", scl)], inc="DW", incn=16)
        hl = em.c("P")
    em.emit("g", lambda e: e.dma_start(out=FK.ap()[slot:slot + 1, :], in_=non[0:1, :]),
            waits=[("DW", em.c("DW"))], inc="DG", incn=16)

    def w_cast_wq(qtr, wsb, bar2):
        for ic in range(32):
            em.emit("g", lambda e, ic=ic, qtr=qtr: e.dma_start(
                out=wsb[:, ic * 1024:(ic + 1) * 1024],
                in_=wq_h[ic * 128:(ic + 1) * 128, qtr * 1024:(qtr + 1) * 1024]),
                    waits=bar2 + [("P", em.c("P"))] if ic == 0 else (),
                    inc="DG", incn=16)
        return [("DG", em.c("DG"))]

    qdr = {}

    def q_drain(qtr, o2, bank, bi, B, t, pwait, bankv):
        dpar = (t * 2 + o2) % 4
        qv = vb("qo", 4096)[:, dpar * 1024:dpar * 1024 + 512]
        em.emit("s", lambda e, bank=bank, qv=qv: e.activation(
            out=qv, in_=pb[bank][:], func=AF.Copy, scale=SCALE),
                waits=[("P", pwait), ("DW", qdr.get(dpar, 0))], inc="Sc")
        bankv[bank] = 0
        scv2 = em.c("Sc")
        lr = (B - own[0]) * 512 + t * 128
        ocol = qtr * 1024 + o2 * 512
        em.emit("y", lambda e, lr=lr, ocol=ocol, qv=qv: e.dma_start(
            out=Qh.ap()[lr:lr + 128, ocol:ocol + 512], in_=qv),
                waits=[("Sc", scv2)], inc="DW", incn=16)
        qdr[dpar] = em.c("DW")
        bankv[("sc", bank)] = scv2

    def weight_pass_q():
        wsb = vb("wsb", 32768)
        hT = [vb("hT0", 16384), vb("hT1", 16384)]
        bar2 = em.barrier()
        bankv = {}
        hloaded = {}
        for qtr in range(4):
            wwait = w_cast_wq(qtr, wsb, bar2)
            for bi, B in enumerate(own):
                par = bi % 2
                hv = hT[par]
                for ic in range(32):
                    em.emit("y", lambda e, B=B, hv=hv, ic=ic: e.dma_start(
                        out=hv[:, ic * 512:(ic + 1) * 512],
                        in_=H2.ap()[B * 512:(B + 1) * 512, ic * 128:(ic + 1) * 128],
                        transpose=True),
                            waits=bar2 + [("P", hloaded.get(par, 0))] if ic == 0 else (),
                            inc="DW", incn=16)
                hdw = em.c("DW")
                for t in range(4):
                    for o2 in range(2):
                        bank = (t * 2 + o2) % 4
                        w0 = wwait + [("DW", hdw), ("Sc", bankv.get(("sc", bank), 0))]
                        for i in range(32):
                            em.emit("t", lambda e, i=i, t=t, o2=o2, bank=bank, hv=hv: e.matmul(
                                out=pb[bank][:],
                                lhsT=hv[:, i * 512 + t * 128:i * 512 + (t + 1) * 128],
                                rhs=wsb[:, i * 1024 + o2 * 512:i * 1024 + (o2 + 1) * 512],
                                start=(i == 0), stop=(i == 31)),
                                    waits=w0 if i == 0 else (), inc="P" if i == 31 else None)
                        q_drain(qtr, o2, bank, bi, B, t, em.c("P"), bankv)
                hloaded[par] = em.c("P")

    weight_pass_q()

    em.q["g"].append(("POLL", FK.ap(), 2, "PL2"))
    em.emit("g", lambda e: e.memset(fs[:], 0), inc="G")
    kpoll_g = em.c("G")
    bar = em.barrier()
    kt = vb("kt", 8192)
    va = vb("va", 8192)
    ktmp = vb("ktmp", 8192)
    for kc in range(64):
        em.emit("y", lambda e, kc=kc: e.dma_start(
            out=va[:, kc * 128:(kc + 1) * 128],
            in_=KV.ap()[kc * 128:(kc + 1) * 128, 128:256]),
                waits=bar + [("G", kpoll_g)] if kc == 0 else (), inc="DW", incn=16)
        em.emit("y", lambda e, kc=kc: e.dma_start(
            out=ktmp[:, kc * 128:(kc + 1) * 128],
            in_=KV.ap()[kc * 128:(kc + 1) * 128, 0:128]),
                inc="DW", incn=16)
    kvdw = em.c("DW")
    scl = em.c("Sc")
    for kc in range(64):
        tp = (kc % 2) * 512
        em.emit("t", lambda e, kc=kc, tp=tp: e.transpose(
            out=pbt[:, tp:tp + 128], in_=ktmp[:, kc * 128:(kc + 1) * 128], identity=idn[:]),
                waits=[("DW", kvdw), ("Sc", em.c("Sc"))],
                inc="P")
        pv = em.c("P")
        em.emit("s", lambda e, kc=kc, tp=tp: e.copy(out=kt[:, kc * 128:(kc + 1) * 128],
                                                    in_=pbt[:, tp:tp + 128]),
                waits=[("P", pv)], inc="Sc")
    ktdone = em.c("Sc")

    albv = vf("alb", 8192)
    mskv = vb("msk", MASKW)
    em.emit("y", lambda e: e.dma_start(out=albv, in_=alb_h), waits=bar, inc="DW", incn=16)
    em.emit("y", lambda e: e.dma_start(out=mskv[:, 0:MASK_ARR.shape[1]],
                                       in_=msk_h[:, 0:MASK_ARR.shape[1]]), inc="DW", incn=16)
    amdw = em.c("DW")

    qt = vb("qt", 16384)
    at = vb("at", 16384)
    ptb = vb("pt", 2 * HG * 512)
    col_check = 0
    atst = 0
    for rb, (B, nch, gl) in enumerate(recs):
        lb = B - own[0]
        for ic in range(32):
            em.emit("y", lambda e, lb=lb, ic=ic: e.dma_start(
                out=qt[:, ic * 512:(ic + 1) * 512],
                in_=Qh.ap()[lb * 512:(lb + 1) * 512, ic * 128:(ic + 1) * 128],
                transpose=True),
                    waits=[("P", em.c("P")), ("Sc", em.c("Sc"))] if ic == 0 else (),
                    inc="DW", incn=16)
        qdw = em.c("DW")
        sfree = {}
        attfree = {}
        denfree = 0
        atw = {}
        for gi, (hg, ch) in enumerate(gl):
            avdone = {}
            exp_lastchunk = {}
            for kc, heads in ch:
                parp = (kc % 2) * HG * 512
                for j, (h, subs) in enumerate(heads):
                    em.emit("t", lambda e, kc=kc, j=j, h=h: e.matmul(
                        out=pb[j][:], lhsT=kt[:, kc * 128:(kc + 1) * 128],
                        rhs=qt[:, h * 512:(h + 1) * 512], start=True, stop=True),
                            waits=[("DW", max(qdw, kvdw, amdw)), ("Sc", max(ktdone, sfree.get(j, 0))),
                                   ("V", max(attfree.get(j, 0), denfree))],
                            inc="P")
                qkp = em.c("P")
                mwaits = []
                for j, (h, subs) in enumerate(heads):
                    base = parp + j * 512
                    for qs, T, kind, colx in subs:
                        o0 = base + (qs - B * 512)
                        if kind == "zero":
                            em.emit("v", lambda e, o0=o0, T=T: e.memset(ptb[:, o0:o0 + T], 0.0),
                                    waits=[("P", avdone.get(kc - 2, 0))], inc="V")
                            mwaits.append(("V", em.c("V")))
                        else:
                            assert colx == col_check, (colx, col_check)
                            col_check += 1
                            em.emit("s", lambda e, j=j, o0=o0, T=T, colx=colx, q0=qs - B * 512: e.activation(
                                out=ptb[:, o0:o0 + T], in_=pb[j][:, q0:q0 + T],
                                func=AF.Exp, bias=albv[:, colx:colx + 1]),
                                    waits=[("P", max(qkp, avdone.get(kc - 2, 0)))], inc="Sc")
                            sfree[j] = em.c("Sc")
                            if kind == "part":
                                moff = MASK_OFF[(T, kc * 128 - qs)]
                                em.emit("v", lambda e, o0=o0, T=T, moff=moff: e.tensor_mul(
                                    out=ptb[:, o0:o0 + T], in0=ptb[:, o0:o0 + T],
                                    in1=mskv[:, moff:moff + T]),
                                        waits=[("Sc", em.c("Sc"))], inc="V")
                                mwaits.append(("V", em.c("V")))
                first = (kc == 0)
                last = (kc == nch - 1)
                avw = [("Sc", em.c("Sc"))] + mwaits
                if first:
                    avw += [("V", max([denfree] + [attfree.get(j, 0) for j in range(HG)]))]
                for j, (h, subs) in enumerate(heads):
                    base = parp + j * 512
                    em.emit("t", lambda e, kc=kc, j=j, base=base, first=first, last=last: e.matmul(
                        out=pb[3 + j][:], lhsT=va[:, kc * 128:(kc + 1) * 128],
                        rhs=ptb[:, base:base + 512], start=first, stop=last),
                            waits=avw if j == 0 else (), inc="P")
                    em.emit("t", lambda e, j=j, base=base, first=first, last=last: e.matmul(
                        out=pb[6][32 * j:32 * j + 1, :], lhsT=ones_bf[:, 0:1],
                        rhs=ptb[:, base:base + 512], start=first, stop=last,
                        tile_position=(0, 32 * j)), inc="P")
                avdone[kc] = em.c("P")
            allp = em.c("P")
            for j, (h, _s) in enumerate(ch[0][1]):
                em.emit("v", lambda e, j=j: e.reciprocal(out=rec[0:1, j * 512:(j + 1) * 512],
                                                         in_=pb[6][32 * j:32 * j + 1, :]),
                        waits=[("P", allp)], inc="V")
                denfree = em.c("V")
                em.emit("t", lambda e, j=j: e.matmul(
                    out=pb[j][:], lhsT=ones_f[0:1, :], rhs=rec[0:1, j * 512:(j + 1) * 512],
                    start=True, stop=True),
                        waits=[("V", denfree), ("Sc", sfree.get(j, 0))], inc="P")
                bcp = em.c("P")
                em.emit("v", lambda e, j=j: e.tensor_copy(out=bc[:], in_=pb[j][:]),
                        waits=[("P", bcp)], inc="V")
                sfree[j] = 0
                attfree[j] = 0
                em.emit("v", lambda e, j=j, h=h: e.tensor_mul(
                    out=at[:, h * 512:(h + 1) * 512], in0=pb[3 + j][:], in1=bc[:]),
                        waits=[("V", em.c("V")), ("DW", atst)], inc="V")
                attfree[j] = em.c("V")
                sfree[j] = 0
        em.emit("y", lambda e, lb=lb: e.dma_start(out=ATh.ap()[lb, :, :], in_=at[:]),
                waits=[("V", em.c("V"))], inc="DW", incn=16)
        atst = em.c("DW")

    def w_cast_wo(qtr, wsb, bar2):
        for ic in range(32):
            em.emit("g", lambda e, ic=ic, qtr=qtr: e.dma_start(
                out=wsb[:, ic * 1024:(ic + 1) * 1024],
                in_=wo_h[ic * 128:(ic + 1) * 128, qtr * 1024:(qtr + 1) * 1024]),
                    waits=bar2 + [("P", em.c("P"))] if ic == 0 else (),
                    inc="DG", incn=16)
        return [("DG", em.c("DG"))]

    def weight_pass_o():
        wsb = vb("wsb", 32768)
        atb = [vb("hT0", 16384), vb("hT1", 16384)]
        bar2 = em.barrier()
        bankv = {}
        hloaded = {}
        dr = ffn_drain(X1.ap(), lambda B: B * 512, X2.ap(), lambda B: (B - own[0]) * 512, 1.0)
        for qtr in range(4):
            wwait = w_cast_wo(qtr, wsb, bar2)
            for bi, B in enumerate(own):
                par = bi % 2
                hv = atb[par]
                lb = B - own[0]
                em.emit("y", lambda e, lb=lb, hv=hv: e.dma_start(out=hv, in_=ATh.ap()[lb, :, :]),
                        waits=bar2 + [("P", hloaded.get(par, 0))], inc="DW", incn=16)
                hdw = em.c("DW")
                for t in range(4):
                    for o2 in range(2):
                        bank = (bi * 8 + t * 2 + o2) % 6
                        w0 = wwait + [("DW", hdw), ("V", bankv.get(bank, 0))]
                        for i in range(32):
                            em.emit("t", lambda e, i=i, t=t, o2=o2, bank=bank, hv=hv: e.matmul(
                                out=pb[bank][:],
                                lhsT=hv[:, i * 512 + t * 128:i * 512 + (t + 1) * 128],
                                rhs=wsb[:, i * 1024 + o2 * 512:i * 1024 + (o2 + 1) * 512],
                                start=(i == 0), stop=(i == 31)),
                                    waits=w0 if i == 0 else (), inc="P" if i == 31 else None)
                        dr(qtr, o2, bank, bi, B, t, em.c("P"), bankv)
                hloaded[par] = em.c("P")

    weight_pass_o()

    nb_own = len(own)
    ln_pass(X2.ap(), H3.ap(), [(lb * 512 + t * 128,) * 2 for lb in range(nb_own) for t in range(4)])
    weight_pass(w_from_whbm, list(range(nb_own)), H3.ap(), lambda lb: lb * 512,
                ffn_drain(X2.ap(), lambda lb: lb * 512, out_h, lambda lb: lb * 512, 0.5))

    do = em.c("DO")
    em.emit("y", lambda e, do=do: e.wait_ge_final(do) if False else e.nop(), waits=[("DO", do)])
    return em


def build_nc():
    nc = bass.Bass("TRN2", target_bir_lowering=False, debug=False)
    R = {"nc": nc}
    R["x_h"] = nc.dram_tensor("x", [S, D], F32, kind="ExternalInput").ap()
    R["wq_h"] = nc.dram_tensor("wq", [D, D], F32, kind="ExternalInput").ap()
    R["wo_h"] = nc.dram_tensor("wo", [D, D], F32, kind="ExternalInput").ap()
    R["wkv_h"] = nc.dram_tensor("wkv", [D, 256], F32, kind="ExternalInput").ap()
    R["pk_h"] = nc.dram_tensor("pk", [D, D // 2], I32, kind="ExternalInput").ap()
    R["sc_h"] = nc.dram_tensor("sc", [D, 1], F32, kind="ExternalInput").ap()
    R["or_h"] = nc.dram_tensor("ortho", [D, D], F32, kind="ExternalInput").ap()
    R["alb_h"] = nc.dram_tensor("alb", [128, NCOL], F32, kind="ExternalInput").ap()
    R["msk_h"] = nc.dram_tensor("msk", [128, MASKW], BF16, kind="ExternalInput").ap()
    R["idn_h"] = nc.dram_tensor("idn", [128, 128], BF16, kind="ExternalInput").ap()
    R["non_h"] = nc.dram_tensor("non", [1, 16], I32, kind="ExternalInput").ap()
    R["out_h"] = nc.dram_tensor("out", [MAXW * 512, D], F32, kind="ExternalOutput").ap()
    R["W_hbm"] = nc.dram_tensor("w_s", [D, D], BF16, kind="Internal", addr_space="Shared")
    R["FW"] = nc.dram_tensor("fw_s", [2, 16], I32, kind="Internal", addr_space="Shared")
    R["KV"] = nc.dram_tensor("kv_s", [S, 256], BF16, kind="Internal", addr_space="Shared")
    R["FK"] = nc.dram_tensor("fk_s", [2, 16], I32, kind="Internal", addr_space="Shared")
    R["H1"] = nc.dram_tensor("h1_s", [S, D], BF16, kind="Internal")
    R["H2"] = nc.dram_tensor("h2_s", [S, D], BF16, kind="Internal")
    R["X1"] = nc.dram_tensor("x1_s", [S, D], F32, kind="Internal")
    R["Qh"] = nc.dram_tensor("q_s", [MAXW * 512, D], BF16, kind="Internal")
    R["ATh"] = nc.dram_tensor("at_s", [MAXW, 128, 32 * 512], BF16, kind="Internal")
    R["X2"] = nc.dram_tensor("x2_s", [MAXW * 512, D], F32, kind="Internal")
    R["H3"] = nc.dram_tensor("h3_s", [MAXW * 512, D], BF16, kind="Internal")

    from contextlib import ExitStack
    with ExitStack() as st:
        sbt = lambda n, shp, dt: st.enter_context(nc.sbuf_tensor(n, shp, dt))
        R["ar"] = sbt("arena", [128, AR_N], BF16)
        R["arf"] = R["ar"][:].bitcast(F32)
        R["idn"] = sbt("idn_sb", [128, 128], BF16)
        R["non"] = sbt("non_sb", [1, 16], I32)
        R["ones_bf"] = sbt("ones_bf", [128, 1], BF16)
        R["ones_f"] = sbt("ones_f", [1, 128], F32)
        R["stt"] = sbt("stt", [128, 16], F32)
        R["rec"] = sbt("rec", [1, 3 * 512], F32)
        R["bc"] = sbt("bc", [128, 512], F32)
        R["fl"] = sbt("fl", [2, 16], I32)
        R["fs"] = sbt("fs", [1, 16], I32)
        R["pb"] = [st.enter_context(nc.psum_tensor(f"pb{i}", [128, 512], F32)) for i in range(7)]
        R["pbt"] = st.enter_context(nc.psum_tensor("pbt", [128, 1024], BF16))
        sems = {}
        for nme in ("DW", "DG", "DO", "P", "V", "Sc", "G", "PL", "PL2"):
            sems[nme] = st.enter_context(nc.semaphore(nme))
        block = st.enter_context(nc.Block())

        emits = [build_core(c, R) for c in range(8)]

        def run_section(e, key, c):
            for fn, ws, inc, incn in emits[c].q[key]:
                for smn, val in ws:
                    e.wait_ge(sems[smn], val)
                inst = fn(e)
                if inc is not None:
                    inst.then_inc(sems[inc], incn)

        @block.sync
        def _(y):
            for c in y.Switch(y.partition_id(), 8):
                run_section(y, "y", c)

        @block.tensor
        def _(t):
            for c in t.Switch(t.partition_id(), 8):
                run_section(t, "t", c)

        @block.vector
        def _(v):
            for c in v.Switch(v.partition_id(), 8):
                run_section(v, "v", c)

        @block.scalar
        def _(s2):
            for c in s2.Switch(s2.partition_id(), 8):
                run_section(s2, "s", c)

        @block.gpsimd
        def _(g):
            for c in g.Switch(g.partition_id(), 8):
                for item in emits[c].q["g"]:
                    if item[0] == "POLL":
                        _poll(nc, g, sems, R, item[1], item[2], item[3])
                        continue
                    fn, ws, inc, incn = item
                    for smn, val in ws:
                        g.wait_ge(sems[smn], val)
                    inst = fn(g)
                    if inc is not None:
                        inst.then_inc(sems[inc], incn)
    return nc


_POLL_STATE = {"n": 0}

def _poll(nc, g, sems, R, flag_ap, mult, plname):
    fl, fs, non = R["fl"], R["fs"], R["non"]
    k = _POLL_STATE["n"] = _POLL_STATE["n"] + 1
    with g.register(f"rc{k}") as rcr, g.register(f"itr{k}") as itr, \
         g.register(f"r_ne{k}") as r_ne, g.register(f"r_lt{k}") as r_lt, \
         g.register(f"r_c{k}") as r_c, g.register(f"tgt{k}") as tgt:
        nv = g.value_load(non[0:1, 0:1])
        g.reg_alu(tgt, nv, mult, ALU.mult)
        nc.reg_mov(rcr, 0)
        nc.reg_mov(itr, 0)
        base = None

        def cond():
            g.nop(cycle_cnt=30000)
            g.dma_start(out=fl[:], in_=flag_ap).then_inc(sems[plname], 16)
            g.scalar_reg_alu(ALU.add, rcr, 16)
            g.wait_ge(sems[plname], rcr)
            with nc.allow_low_precision(reason="int flag sum"):
                g.tensor_reduce(out=fs[:], in_=fl[:], axis=AX.C, op=ALU.add)
            sv = g.value_load(fs[0:1, 0:1])
            g.scalar_reg_alu(ALU.add, itr, 1)
            g.reg_alu(r_ne, sv, tgt, ALU.not_equal)
            g.reg_alu(r_lt, itr, 400000, ALU.is_lt)
            g.reg_alu(r_c, r_ne, r_lt, ALU.logical_and)
            return r_c

        with g.While(cond):
            g.nop()


def kernel(L0_attn_hidden_states, L0_attn_Wq, L0_attn_Wkv, L0_attn_Wo,
           L0_ffn_base_packed, L0_ffn_scales, L0_ffn_w_ortho):
    x = np.ascontiguousarray(L0_attn_hidden_states.reshape(S, D).astype(np.float32))
    wq = np.ascontiguousarray(np.asarray(L0_attn_Wq, dtype=np.float32))
    wkv = np.ascontiguousarray(np.asarray(L0_attn_Wkv, dtype=np.float32))
    wo = np.ascontiguousarray(np.asarray(L0_attn_Wo, dtype=np.float32))
    pk = np.ascontiguousarray(np.asarray(L0_ffn_base_packed, dtype=np.int32))
    sc = np.ascontiguousarray(np.asarray(L0_ffn_scales, dtype=np.float32))
    orth = np.ascontiguousarray(np.asarray(L0_ffn_w_ortho, dtype=np.float32))
    idn = np.eye(128, dtype=ml_dtypes.bfloat16)
    mskp = np.zeros((128, MASKW), dtype=ml_dtypes.bfloat16)
    mskp[:, :MASK_ARR.shape[1]] = MASK_ARR
    nonce = int(np.random.randint(1, 1 << 28))
    non = np.full((1, 16), nonce, dtype=np.int32)

    nc = build_nc()
    in_maps = []
    for c in range(8):
        in_maps.append(dict(x=x, wq=wq, wo=wo, wkv=wkv, pk=pk, sc=sc, ortho=orth,
                            alb=build_alibi(c), msk=mskp, idn=idn, non=non))
    res = run_bass_kernel_spmd(nc, in_maps, core_ids=list(range(8)))
    out = np.empty((S, D), dtype=np.float32)
    for c in range(8):
        o = res.results[c]["out"]
        w = len(OWN[c]) * 512
        out[OWN[c][0] * 512:OWN[c][0] * 512 + w, :] = o[:w, :]
    return out.reshape(1, S, D)


if __name__ == "__main__":
    print("E_BOUND", E_BOUND)
    print("OWN", OWN)
    print("PARTS", PARTS)
    print("MASKW", MASKW, "MAXW", MAXW)
